# version 04
# baseline (speedup 1.0000x reference)
import math
from contextlib import ExitStack

import numpy as np

import concourse.bass as bass
import concourse.mybir as mybir
from concourse.bass_utils import run_bass_kernel_spmd

F32 = mybir.dt.float32
BF16 = mybir.dt.bfloat16
I32 = mybir.dt.int32
I8 = mybir.dt.int8
AF = mybir.ActivationFunctionType
ALU = mybir.AluOpType
AX = mybir.AxisListType

P = 128
T = 2048
TT = T // P
D = 4096
DC = D // P
NH = 32
HD = 128
FH = 11008
FC = FH // P
EPS = 1e-5
SCL = 1.0 / math.sqrt(HD)
NEG = -1.0e30

SEMS = ("L", "G", "Pe", "Ac", "Ve", "Po")


def _phase_barrier(nc, block, tot):
    vals = [(tot.sem[s], tot.c[s] * (16 if s in ("L", "G") else 1)) for s in SEMS]

    def body(eng):
        for sem, v in vals:
            if v > 0:
                eng.wait_ge(sem, v)

    block.sync(body)
    block.gpsimd(body)
    block.tensor(body)
    block.scalar(body)
    block.vector(body)


class Tot:

    def __init__(self, nc, es):
        self.sem = {s: es.enter_context(nc.semaphore(s)) for s in SEMS}
        self.c = {s: 0 for s in SEMS}

    def v(self, s, n):
        return n * 16 if s in ("L", "G") else n


def build(n_dbg_phases=99):
    nc = bass.Bass("TRN2", target_bir_lowering=False)

    x_in = nc.dram_tensor("x", [T, D], F32, kind="ExternalInput").ap()
    w_in = {m: nc.dram_tensor(f"w{m}", [D, D], F32, kind="ExternalInput").ap()
            for m in "qkvo"}
    b_in = {m: nc.dram_tensor(f"b{m}", [D], F32, kind="ExternalInput").ap()
            for m in "qkvo"}
    wf_in = {i: nc.dram_tensor(f"wf{i}", [352256, 64], I8, kind="ExternalInput").ap()
             for i in (1, 2, 3)}
    sf_in = {i: nc.dram_tensor(f"sf{i}", [704512], F32, kind="ExternalInput").ap()
             for i in (1, 2, 3)}
    out_t = nc.dram_tensor("out", [T, D], F32, kind="ExternalOutput")
    out_ap = out_t.ap()

    def scratch(name, shape, dtype, dbg_phase):
        kind = "ExternalOutput" if dbg_phase == n_dbg_phases else "Internal"
        return nc.dram_tensor(name, shape, dtype, kind=kind).ap()

    qkv_s = scratch("qkv_s", [3, DC, P, T], BF16, 2)
    o_s = scratch("o_s", [NH, P, T], BF16, 3)
    x2_s = scratch("x2_s", [T, D], F32, 4)
    h_s = scratch("h_s", [FC, P, T], BF16, 6)
    w2t_s = scratch("w2t_s", [DC, P, FH], BF16, 7)
    xn1_dbg = scratch("xn1_dbg", [DC, P, T], BF16, 1) if n_dbg_phases <= 1 else None
    xn2_dbg = scratch("xn2_dbg", [DC, P, T], BF16, 5) if n_dbg_phases <= 5 else None

    es = ExitStack()
    tot = Tot(nc, es)
    L, G, Pe, Ac, Ve, Po = (tot.sem[s] for s in SEMS)

    identF = es.enter_context(nc.sbuf_tensor("identF", [P, P], F32))
    identB = es.enter_context(nc.sbuf_tensor("identB", [P, P], BF16))
    maskF = es.enter_context(nc.sbuf_tensor("maskF", [P, P], F32))
    b_all = es.enter_context(nc.sbuf_tensor("b_all", [P, 4 * NH], F32))
    bqs = es.enter_context(nc.sbuf_tensor("bqs", [P, NH], F32))
    stats_s = es.enter_context(nc.sbuf_tensor("stats_s", [P, TT * DC], F32))
    stats_q = es.enter_context(nc.sbuf_tensor("stats_q", [P, TT * DC], F32))
    sm_ = es.enter_context(nc.sbuf_tensor("smalls", [P, 16], F32))
    ssum, ssq, mean_, var_, rstd, rcp = (sm_[:, i:i + 1] for i in range(6))

    with nc.named_scope("pro"), nc.Block() as blk, nc.sbuf_tensor("bload", [32, 4 * P], F32) as bload:
        with nc.psum_tensor("ppro", [P, P], F32) as ppro:
            @blk.sync
            def _(sy):
                for mi, m in enumerate("qkvo"):
                    sy.dma_start(bload[0:32, mi * P:(mi + 1) * P],
                                 b_in[m].rearrange("(h p) -> h p", p=P)).then_inc(L, 16)

            @blk.gpsimd
            def _(g):
                g.iota(maskF[:, :], pattern=[[1, P]], base=0, channel_multiplier=-1,
                       allow_small_or_imprecise_dtypes=True).then_inc(Po, 1)

            @blk.vector
            def _(v):
                v.wait_ge(Po, 1)
                v.tensor_single_scalar(identF[:, :], maskF[:, :], 0.0, ALU.is_equal)
                v.drain()
                v.tensor_copy(identB[:, :], identF[:, :])
                v.tensor_single_scalar(maskF[:, :], maskF[:, :], 0.0, ALU.is_gt)
                v.drain()
                v.tensor_single_scalar(maskF[:, :], maskF[:, :], NEG, ALU.mult).then_inc(Ve, 1)

            @blk.tensor
            def _(t):
                t.wait_ge(Ve, 1)
                t.wait_ge(L, 16 * 4)
                for mi in range(4):
                    ins = t.transpose(ppro[:, mi * 32:(mi + 1) * 32],
                                      bload[0:32, mi * P:(mi + 1) * P],
                                      identF[0:32, 0:32])
                ins.then_inc(Pe, 1)

            @blk.vector
            def _(v):
                v.wait_ge(Pe, 1)
                v.tensor_copy(b_all[:, :], ppro[:, :])
                v.drain()
                v.tensor_single_scalar(bqs[:, :], b_all[:, 0:NH], SCL, ALU.mult).then_inc(Ve, 1)

        tot.c["L"] += 4
        tot.c["Po"] += 1
        tot.c["Ve"] += 2
        tot.c["Pe"] += 1
        _phase_barrier(nc, blk, tot)

    es_xn1 = ExitStack()
    xn1 = es_xn1.enter_context(nc.sbuf_tensor("xn1", [P, DC * T], BF16))

    B0 = dict(tot.c)
    with nc.named_scope("a1"), nc.Block() as blk, ExitStack() as pes:
        xtile = [pes.enter_context(nc.sbuf_tensor(f"xt{i}", [P, D], F32)) for i in range(2)]
        xn_tm = [pes.enter_context(nc.sbuf_tensor(f"xn_tm{i}", [P, D], BF16)) for i in range(2)]
        scr4k = pes.enter_context(nc.sbuf_tensor("scr4k", [P, D], BF16))
        ptr = [pes.enter_context(nc.psum_tensor(f"ptrA{i}", [P, 512], BF16)) for i in range(2)]
        vat = lambda tt, k: B0["Ve"] + tt * 11 + k
        aat = lambda tt, k: B0["Ac"] + tt * 2 + k
        pat = lambda tt, k: B0["Pe"] + tt * 8 + k

        @blk.sync
        def _(sy):
            for tt in range(TT):
                if tt >= 2:
                    sy.wait_ge(Ve, vat(tt - 2, 3))
                    sy.wait_ge(Ac, aat(tt - 2, 1))
                sy.dma_start(xtile[tt % 2][:, :], x_in[tt * P:(tt + 1) * P, :]).then_inc(L, 16)

        @blk.vector
        def _(v):
            for tt in range(TT):
                v.wait_ge(L, 16 * (B0["L"] + tt + 1))
                v.reduce_sum(ssum, xtile[tt % 2][:, :], axis=AX.X).then_inc(Ve, 1)
                v.wait_ge(Ac, aat(tt, 1))
                v.drain()
                v.tensor_scalar_mul(mean_, ssum, 1.0 / D)
                v.drain()
                v.tensor_mul(var_, mean_, mean_)
                v.drain()
                v.scalar_tensor_tensor(var_, ssq, 1.0 / D, var_, ALU.mult, ALU.subtract)
                v.drain()
                v.tensor_scalar_add(var_, var_, EPS).then_inc(Ve, 1)
                v.wait_ge(Ac, aat(tt, 2))
                v.reciprocal(rstd, rstd)
                v.drain()
                if tt >= 2:
                    v.wait_ge(Pe, pat(tt - 2, 8))
                v.tensor_scalar(xn_tm[tt % 2][:, :], xtile[tt % 2][:, :], mean_, rstd,
                                ALU.subtract, ALU.mult).then_inc(Ve, 1)
                for g4 in range(8):
                    v.wait_ge(Pe, pat(tt, g4 + 1))
                    dst = bass.AP(xn1, (4 * g4) * T + tt * P, [[DC * T, P], [T, 4], [1, P]])
                    v.tensor_copy(dst, ptr[g4 % 2].ap().rearrange("p (j c) -> p j c", c=P)
                                  ).then_inc(Ve, 1)

        @blk.scalar
        def _(s):
            for tt in range(TT):
                s.wait_ge(L, 16 * (B0["L"] + tt + 1))
                s.activation(scr4k[:, :], xtile[tt % 2][:, :], AF.Square,
                             accum_out=ssq).then_inc(Ac, 1)
                s.wait_ge(Ve, vat(tt, 2))
                s.sqrt(rstd, var_).then_inc(Ac, 1)

        @blk.tensor
        def _(t):
            for tt in range(TT):
                t.wait_ge(Ve, vat(tt, 3))
                for g4 in range(8):
                    gg = tt * 8 + g4
                    if gg >= 2:
                        prev_tt, prev_g4 = divmod(gg - 2, 8)
                        t.wait_ge(Ve, vat(prev_tt, 4 + prev_g4))
                    for j in range(4):
                        dcb = g4 * 4 + j
                        ins = t.transpose(ptr[g4 % 2][:, j * P:(j + 1) * P],
                                          xn_tm[tt % 2][:, dcb * P:(dcb + 1) * P],
                                          identB[:, :])
                    ins.then_inc(Pe, 1)

        tot.c["L"] += TT
        tot.c["Ve"] += TT * 11
        tot.c["Ac"] += TT * 2
        tot.c["Pe"] += TT * 8
        _phase_barrier(nc, blk, tot)

    if n_dbg_phases <= 1:
        with nc.Block() as blk:
            @blk.sync
            def _(sy):
                for dc in range(DC):
                    sy.dma_start(xn1_dbg[dc], xn1[:, dc * T:(dc + 1) * T]).then_inc(L, 16)
                sy.wait_ge(L, 16 * (tot.c["L"] + DC))
            tot.c["L"] += DC
        return nc

    B0 = dict(tot.c)
    NI = 3 * DC
    with nc.named_scope("a2"), nc.Block() as blk, ExitStack() as pes:
        wlhs = [pes.enter_context(nc.sbuf_tensor(f"wlhs{i}", [P, D], BF16)) for i in range(2)]
        qkv_sb = [pes.enter_context(nc.sbuf_tensor(f"qkvsb{i}", [P, T], BF16)) for i in range(2)]
        pa = [pes.enter_context(nc.psum_tensor(f"paB{i}", [P, 512], F32)) for i in range(8)]
        mats = [w_in["q"], w_in["k"], w_in["v"]]

        @blk.gpsimd
        def _(g):
            for idx in range(NI):
                m, oc = divmod(idx, DC)
                if idx >= 2:
                    g.wait_ge(Pe, B0["Pe"] + (idx - 1) * 4)
                src = bass.AP(mats[m].tensor, (oc * P) * D, [[D, P], [1, D]])
                g.dma_start(wlhs[idx % 2][:, :], src).then_inc(G, 16)

        @blk.tensor
        def _(t):
            for idx in range(NI):
                t.wait_ge(G, 16 * (B0["G"] + idx + 1))
                if idx >= 2:
                    t.wait_ge(Ac, B0["Ac"] + (idx - 1) * 4)
                q = 4 * (idx % 2)
                for dc in range(DC):
                    for tc in range(4):
                        ins = t.matmul(pa[q + tc][:, :], wlhs[idx % 2][:, dc * P:(dc + 1) * P],
                                       xn1[:, dc * T + tc * 512:dc * T + (tc + 1) * 512],
                                       start=(dc == 0), stop=(dc == DC - 1))
                        if dc == DC - 1:
                            ins.then_inc(Pe, 1)

        @blk.scalar
        def _(s):
            for idx in range(NI):
                m, oc = divmod(idx, DC)
                q = 4 * (idx % 2)
                bias = bqs[:, oc:oc + 1] if m == 0 else b_all[:, m * NH + oc:m * NH + oc + 1]
                scale = SCL if m == 0 else 1.0
                for tc in range(4):
                    s.wait_ge(Pe, B0["Pe"] + idx * 4 + tc + 1)
                    if tc == 0 and idx >= 2:
                        s.wait_ge(L, 16 * (B0["L"] + idx - 1))
                    s.activation(qkv_sb[idx % 2][:, tc * 512:(tc + 1) * 512], pa[q + tc][:, :],
                                 AF.Identity, bias=bias, scale=scale).then_inc(Ac, 1)

        @blk.sync
        def _(sy):
            for idx in range(NI):
                m, oc = divmod(idx, DC)
                sy.wait_ge(Ac, B0["Ac"] + 4 * (idx + 1))
                sy.dma_start(qkv_s[m, oc], qkv_sb[idx % 2][:, :]).then_inc(L, 16)

        tot.c["L"] += NI
        tot.c["G"] += NI
        tot.c["Pe"] += 4 * NI
        tot.c["Ac"] += 4 * NI
        _phase_barrier(nc, blk, tot)

    es_xn1.close()
    if n_dbg_phases <= 2:
        return nc

    B0 = dict(tot.c)
    ntg = lambda i: (i + 4) // 4

    pe_rel, ac_rel, ve_rel = {}, {}, {}
    c = 0
    for g in range(4):
        c += 1
        pe_rel[("vtm", g)] = c
    c += 1
    pe_rel[("sc", 0)] = c
    for i in range(16):
        if i < 15:
            c += 1
            pe_rel[("sc", i + 1)] = c
        for k in range(ntg(i)):
            c += 1
            pe_rel[("tr", i, k)] = c
        if i % 4 == 3:
            c += 1
            pe_rel[("pv", i // 4)] = c
    PH_P = c
    c = 0
    for i in range(16):
        c += 1
        ac_rel[("exp", i)] = c
        for k in range(ntg(i)):
            c += 1
            ac_rel[("ev", i, k)] = c
        if i % 4 == 3:
            c += 1
            ac_rel[("oev", i // 4)] = c
    PH_A = c
    c = 0
    for g in range(4):
        c += 1
        ve_rel[("vev", g)] = c
    for i in range(16):
        c += 1
        ve_rel[("mask", i)] = c
        c += 1
        ve_rel[("red", i)] = c
        c += 1
        ve_rel[("mult", i)] = c
    PH_V = c
    tg_evac = [("Ve", ve_rel[("vev", g)]) for g in range(4)]
    tg_of = {}
    tgc = 4
    for i in range(16):
        for k in range(ntg(i)):
            tg_of[(i, k)] = tgc
            tg_evac.append(("Ac", ac_rel[("ev", i, k)]))
            tgc += 1
    NTG_H = tgc

    with nc.named_scope("a3"), nc.Block() as blk, ExitStack() as pes:
        qh = [pes.enter_context(nc.sbuf_tensor(f"qh{i}", [P, T], BF16)) for i in range(2)]
        kh = [pes.enter_context(nc.sbuf_tensor(f"kh{i}", [P, T], BF16)) for i in range(2)]
        vh = [pes.enter_context(nc.sbuf_tensor(f"vh{i}", [P, T], BF16)) for i in range(2)]
        vtm = pes.enter_context(nc.sbuf_tensor("vtm", [P, T], BF16))
        pf32 = pes.enter_context(nc.sbuf_tensor("pf32", [P, T], F32))
        pnbf = pes.enter_context(nc.sbuf_tensor("pnbf", [P, T], BF16))
        attT = pes.enter_context(nc.sbuf_tensor("attT", [P, 16 * 512], BF16))
        oh = [pes.enter_context(nc.sbuf_tensor(f"oh{i}", [P, T], BF16)) for i in range(2)]
        ps_s = pes.enter_context(nc.psum_tensor("ps_s", [P, 2048], F32))
        ptr = [pes.enter_context(nc.psum_tensor(f"ptrC{i}", [P, 512], BF16)) for i in range(2)]
        ps_o = pes.enter_context(nc.psum_tensor("ps_o", [P, 512], F32))
        nm = sm_[:, 6:7]

        def tr_war(t, h, tg):
            gtg = h * NTG_H + tg - 2
            if gtg < 0:
                return
            ph, ptg = divmod(gtg, NTG_H)
            sem_name, rel = tg_evac[ptg]
            if sem_name == "Ve":
                t.wait_ge(Ve, B0["Ve"] + ph * PH_V + rel)
            else:
                t.wait_ge(Ac, B0["Ac"] + ph * PH_A + rel)

        @blk.sync
        def _(sy):
            for h in range(NH):
                if h >= 2:
                    sy.wait_ge(Pe, B0["Pe"] + (h - 1) * PH_P)
                for mi, buf in ((0, qh), (1, kh), (2, vh)):
                    sy.dma_start(buf[h % 2][:, :], qkv_s[mi, h]).then_inc(L, 16)

        @blk.tensor
        def _(t):
            for h in range(NH):
                hp, ha, hv = B0["Pe"] + h * PH_P, B0["Ac"] + h * PH_A, B0["Ve"] + h * PH_V
                t.wait_ge(L, 16 * (B0["L"] + 3 * h + 3))
                for g in range(4):
                    tr_war(t, h, g)
                    for j in range(4):
                        b = g * 4 + j
                        ins = t.transpose(ptr[g % 2][:, j * P:(j + 1) * P],
                                          vh[h % 2][:, b * P:(b + 1) * P], identB[:, :])
                    ins.then_inc(Pe, 1)

                def emit_sc(i):
                    if i == 0:
                        if h > 0:
                            t.wait_ge(Ac, B0["Ac"] + (h - 1) * PH_A + ac_rel[("exp", 15)])
                    else:
                        t.wait_ge(Ac, ha + ac_rel[("exp", i - 1)])
                    nk = (i + 1) * P
                    for s0 in range(0, nk, 512):
                        w = min(512, nk - s0)
                        ins = t.matmul(ps_s[:, s0:s0 + w], qh[h % 2][:, i * P:(i + 1) * P],
                                       kh[h % 2][:, s0:s0 + w], start=True, stop=True)
                    ins.then_inc(Pe, 1)

                emit_sc(0)
                for i in range(16):
                    if i < 15:
                        emit_sc(i + 1)
                    t.wait_ge(Ve, hv + ve_rel[("mult", i)])
                    for k in range(ntg(i)):
                        tr_war(t, h, tg_of[(i, k)])
                        nb = min(4, i + 1 - 4 * k)
                        for j in range(nb):
                            b = 4 * k + j
                            ins = t.transpose(ptr[tg_of[(i, k)] % 2][:, j * P:(j + 1) * P],
                                              pnbf[:, b * P:(b + 1) * P], identB[:, :])
                        ins.then_inc(Pe, 1)
                    if i % 4 == 3:
                        qg = i // 4
                        t.wait_ge(Ac, ha + ac_rel[("ev", i, ntg(i) - 1)])
                        if qg > 0:
                            t.wait_ge(Ac, ha + ac_rel[("oev", qg - 1)])
                        elif h > 0:
                            t.wait_ge(Ac, B0["Ac"] + (h - 1) * PH_A + ac_rel[("oev", 3)])
                        t.wait_ge(Ve, hv + ve_rel[("vev", 3)])
                        for j in range(4 * qg + 4):
                            off = max(0, j - 4 * qg) * P
                            ins = t.matmul(ps_o[:, off:512], vtm[:, j * P:(j + 1) * P],
                                           attT[:, j * 512 + off:(j + 1) * 512],
                                           start=(j == 0), stop=(j == 4 * qg + 3))
                        ins.then_inc(Pe, 1)

        @blk.vector
        def _(v):
            for h in range(NH):
                hp, ha, hv = B0["Pe"] + h * PH_P, B0["Ac"] + h * PH_A, B0["Ve"] + h * PH_V
                for g in range(4):
                    v.wait_ge(Pe, hp + pe_rel[("vtm", g)])
                    src = ptr[g % 2].ap().rearrange("p (j c) -> p j c", c=P)
                    dst = bass.AP(vtm, g * 512, [[T, P], [P, 4], [1, P]])
                    v.tensor_copy(dst, src).then_inc(Ve, 1)
                for i in range(16):
                    nk = (i + 1) * P
                    v.wait_ge(Pe, hp + pe_rel[("sc", i)])
                    v.tensor_tensor(ps_s[:, i * P:nk], ps_s[:, i * P:nk], maskF[:, :],
                                    ALU.add).then_inc(Ve, 1)
                    v.drain()
                    v.reduce_max(nm, ps_s[:, 0:nk], axis=AX.X, negate=True).then_inc(Ve, 1)
                    v.wait_ge(Ac, ha + ac_rel[("exp", i)])
                    v.reciprocal(rcp, ssum)
                    v.drain()
                    if i > 0:
                        v.wait_ge(Pe, hp + pe_rel[("tr", i - 1, ntg(i - 1) - 1)])
                    v.tensor_scalar(pnbf[:, 0:nk], pf32[:, 0:nk], rcp, None,
                                    ALU.mult).then_inc(Ve, 1)

        @blk.scalar
        def _(s):
            for h in range(NH):
                hp, ha, hv = B0["Pe"] + h * PH_P, B0["Ac"] + h * PH_A, B0["Ve"] + h * PH_V
                for i in range(16):
                    nk = (i + 1) * P
                    s.wait_ge(Ve, hv + ve_rel[("red", i)])
                    s.activation(pf32[:, 0:nk], ps_s[:, 0:nk], AF.Exp, bias=nm,
                                 scale=1.0, accum_out=ssum).then_inc(Ac, 1)
                    for k in range(ntg(i)):
                        s.wait_ge(Pe, hp + pe_rel[("tr", i, k)])
                        qg = i // 4
                        if qg > 0:
                            s.wait_ge(Pe, hp + pe_rel[("pv", qg - 1)])
                        elif h > 0:
                            s.wait_ge(Pe, B0["Pe"] + h * PH_P)
                        nb = min(4, i + 1 - 4 * k)
                        src = ptr[tg_of[(i, k)] % 2].ap().rearrange("p (j c) -> p j c", c=P)
                        src = bass.AP(src.tensor, src.offset, [[512, P], [P, nb], [1, P]])
                        dst = bass.AP(attT, (4 * k) * 512 + (i % 4) * P,
                                      [[16 * 512, P], [512, nb], [1, P]])
                        s.activation(dst, src, AF.Copy).then_inc(Ac, 1)
                    if i % 4 == 3:
                        qg = i // 4
                        s.wait_ge(Pe, hp + pe_rel[("pv", qg)])
                        if qg == 0 and h >= 2:
                            s.wait_ge(G, 16 * (B0["G"] + h - 1))
                        s.activation(oh[h % 2][:, qg * 512:(qg + 1) * 512], ps_o[:, :],
                                     AF.Copy).then_inc(Ac, 1)

        @blk.gpsimd
        def _(g):
            for h in range(NH):
                g.wait_ge(Ac, B0["Ac"] + h * PH_A + ac_rel[("oev", 3)])
                g.dma_start(o_s[h], oh[h % 2][:, :]).then_inc(G, 16)

        tot.c["L"] += 3 * NH
        tot.c["G"] += NH
        tot.c["Pe"] += PH_P * NH
        tot.c["Ac"] += PH_A * NH
        tot.c["Ve"] += PH_V * NH
        _phase_barrier(nc, blk, tot)

    if n_dbg_phases <= 3:
        return nc

    B0 = dict(tot.c)
    es_ores = ExitStack()
    o_res = es_ores.enter_context(nc.sbuf_tensor("o_res", [P, DC * T], BF16))
    with nc.named_scope("b"), nc.Block() as blk, ExitStack() as pes:
        wlhsB = [pes.enter_context(nc.sbuf_tensor(f"wlhsB{i}", [P, D], BF16)) for i in range(2)]
        ao = pes.enter_context(nc.sbuf_tensor("ao", [P, T], F32))
        xcol = [pes.enter_context(nc.sbuf_tensor(f"xcol{i}", [P, T], F32)) for i in range(2)]
        x2sb = [pes.enter_context(nc.sbuf_tensor(f"x2sb{i}", [P, T], F32)) for i in range(2)]
        scr128 = pes.enter_context(nc.sbuf_tensor("scr128", [P, P], BF16))
        ptf = [pes.enter_context(nc.psum_tensor(f"ptfD{i}", [P, 512], F32)) for i in range(2)]
        pa = [pes.enter_context(nc.psum_tensor(f"paD{i}", [P, 512], F32)) for i in range(4)]

        def g_ld(oc):
            return B0["G"] + (2 * oc if oc >= 1 else 1)

        def g_st(oc):
            return B0["G"] + (2 * oc + 3 if oc < DC - 1 else 2 * DC)

        @blk.sync
        def _(sy):
            for h in range(DC):
                sy.dma_start(o_res[:, h * T:(h + 1) * T], o_s[h]).then_inc(L, 16)
            for oc in range(DC):
                if oc >= 2:
                    sy.wait_ge(Ve, B0["Ve"] + (oc - 2) * 20 + 4)
                src = bass.AP(x_in.tensor, oc * P, [[D, P], [P * D, TT], [1, P]])
                sy.dma_start(xcol[oc % 2].ap().rearrange("p (tt c) -> p tt c", c=P),
                             src).then_inc(L, 16)

        @blk.gpsimd
        def _(g):
            for oc in range(DC):
                if oc >= 2:
                    g.wait_ge(Pe, B0["Pe"] + (oc - 2) * 8 + 4)
                g.dma_start(wlhsB[oc % 2][:, :], w_in["o"][oc * P:(oc + 1) * P, :]
                            ).then_inc(G, 16)
                if oc >= 1:
                    g.wait_ge(Ve, B0["Ve"] + (oc - 1) * 20 + 4)
                    poc = oc - 1
                    dst = bass.AP(x2_s.tensor, poc * P, [[D, P], [P * D, TT], [1, P]])
                    g.dma_start(dst, x2sb[poc % 2].ap().rearrange("p (tt c) -> p tt c", c=P)
                                ).then_inc(G, 16)
            g.wait_ge(Ve, B0["Ve"] + (DC - 1) * 20 + 4)
            dst = bass.AP(x2_s.tensor, (DC - 1) * P, [[D, P], [P * D, TT], [1, P]])
            g.dma_start(dst, x2sb[(DC - 1) % 2].ap().rearrange("p (tt c) -> p tt c", c=P)
                        ).then_inc(G, 16)

        @blk.tensor
        def _(t):
            t.wait_ge(L, 16 * (B0["L"] + DC))
            for oc in range(DC):
                t.wait_ge(G, 16 * g_ld(oc))
                if oc >= 1:
                    t.wait_ge(Ac, B0["Ac"] + (oc - 1) * 20 + 4)
                for dc in range(DC):
                    for tc in range(4):
                        ins = t.matmul(pa[tc][:, :], wlhsB[oc % 2][:, dc * P:(dc + 1) * P],
                                       o_res[:, dc * T + tc * 512:dc * T + (tc + 1) * 512],
                                       start=(dc == 0), stop=(dc == DC - 1))
                        if dc == DC - 1:
                            ins.then_inc(Pe, 1)
                for g4 in range(4):
                    t.wait_ge(Ac, B0["Ac"] + oc * 20 + g4 + 1)
                    gg = oc * 4 + g4
                    if gg >= 2:
                        pi, pg = divmod(gg - 2, 4)
                        t.wait_ge(Ve, B0["Ve"] + pi * 20 + pg + 1)
                    for j in range(4):
                        b = g4 * 4 + j
                        ins = t.transpose(ptf[g4 % 2][:, j * P:(j + 1) * P],
                                          ao[:, b * P:(b + 1) * P], identF[:, :])
                    ins.then_inc(Pe, 1)

        @blk.vector
        def _(v):
            for oc in range(DC):
                for g4 in range(4):
                    v.wait_ge(Pe, B0["Pe"] + oc * 8 + 4 + g4 + 1)
                    v.wait_ge(L, 16 * (B0["L"] + DC + oc + 1))
                    if oc >= 2:
                        v.wait_ge(G, 16 * g_st(oc - 2))
                        v.wait_ge(Ac, B0["Ac"] + (oc - 2) * 20 + 20)
                    v.tensor_add(x2sb[oc % 2][:, g4 * 512:(g4 + 1) * 512], ptf[g4 % 2][:, :],
                                 xcol[oc % 2][:, g4 * 512:(g4 + 1) * 512]).then_inc(Ve, 1)
                for tt in range(TT):
                    v.reduce_sum(stats_s[:, tt * DC + oc:tt * DC + oc + 1],
                                 x2sb[oc % 2][:, tt * P:(tt + 1) * P], axis=AX.X).then_inc(Ve, 1)

        @blk.scalar
        def _(s):
            for oc in range(DC):
                bias = b_all[:, 3 * NH + oc:3 * NH + oc + 1]
                for tc in range(4):
                    s.wait_ge(Pe, B0["Pe"] + oc * 8 + tc + 1)
                    if tc == 0 and oc >= 1:
                        s.wait_ge(Pe, B0["Pe"] + (oc - 1) * 8 + 8)
                    s.activation(ao[:, tc * 512:(tc + 1) * 512], pa[tc][:, :],
                                 AF.Identity, bias=bias, scale=1.0).then_inc(Ac, 1)
                for tt in range(TT):
                    s.wait_ge(Ve, B0["Ve"] + oc * 20 + tt // 4 + 1)
                    s.activation(scr128[:, :], x2sb[oc % 2][:, tt * P:(tt + 1) * P], AF.Square,
                                 accum_out=stats_q[:, tt * DC + oc:tt * DC + oc + 1]
                                 ).then_inc(Ac, 1)

        tot.c["L"] += 2 * DC
        tot.c["G"] += 2 * DC
        tot.c["Pe"] += 8 * DC
        tot.c["Ve"] += 20 * DC
        tot.c["Ac"] += 20 * DC
        _phase_barrier(nc, blk, tot)

    es_ores.close()
    if n_dbg_phases <= 4:
        return nc

    B0 = dict(tot.c)
    es_xn2 = ExitStack()
    xn2 = es_xn2.enter_context(nc.sbuf_tensor("xn2", [P, DC * T], BF16))
    with nc.named_scope("c0"), nc.Block() as blk, ExitStack() as pes:
        xtile = [pes.enter_context(nc.sbuf_tensor(f"xtC{i}", [P, D], F32)) for i in range(2)]
        xn_tm = [pes.enter_context(nc.sbuf_tensor(f"xn_tmC{i}", [P, D], BF16)) for i in range(2)]
        ptr = [pes.enter_context(nc.psum_tensor(f"ptrE{i}", [P, 512], BF16)) for i in range(2)]
        vat = lambda tt, k: B0["Ve"] + tt * 10 + k
        aat = lambda tt: B0["Ac"] + tt + 1
        pat = lambda tt, k: B0["Pe"] + tt * 8 + k

        @blk.sync
        def _(sy):
            for tt in range(TT):
                if tt >= 2:
                    sy.wait_ge(Ve, vat(tt - 2, 2))
                sy.dma_start(xtile[tt % 2][:, :], x2_s[tt * P:(tt + 1) * P, :]).then_inc(L, 16)

        @blk.vector
        def _(v):
            for tt in range(TT):
                v.reduce_sum(ssum, stats_s[:, tt * DC:(tt + 1) * DC], axis=AX.X)
                v.reduce_sum(ssq, stats_q[:, tt * DC:(tt + 1) * DC], axis=AX.X)
                v.drain()
                v.tensor_scalar_mul(mean_, ssum, 1.0 / D)
                v.drain()
                v.tensor_mul(var_, mean_, mean_)
                v.drain()
                v.scalar_tensor_tensor(var_, ssq, 1.0 / D, var_, ALU.mult, ALU.subtract)
                v.drain()
                v.tensor_scalar_add(var_, var_, EPS).then_inc(Ve, 1)
                v.wait_ge(Ac, aat(tt))
                v.reciprocal(rstd, rstd)
                v.drain()
                v.wait_ge(L, 16 * (B0["L"] + tt + 1))
                if tt >= 2:
                    v.wait_ge(Pe, pat(tt - 2, 8))
                v.tensor_scalar(xn_tm[tt % 2][:, :], xtile[tt % 2][:, :], mean_, rstd,
                                ALU.subtract, ALU.mult).then_inc(Ve, 1)
                for g4 in range(8):
                    v.wait_ge(Pe, pat(tt, g4 + 1))
                    dst = bass.AP(xn2, (4 * g4) * T + tt * P, [[DC * T, P], [T, 4], [1, P]])
                    v.tensor_copy(dst, ptr[g4 % 2].ap().rearrange("p (j c) -> p j c", c=P)
                                  ).then_inc(Ve, 1)

        @blk.scalar
        def _(s):
            for tt in range(TT):
                s.wait_ge(Ve, vat(tt, 1))
                s.sqrt(rstd, var_).then_inc(Ac, 1)

        @blk.tensor
        def _(t):
            for tt in range(TT):
                t.wait_ge(Ve, vat(tt, 2))
                for g4 in range(8):
                    gg = tt * 8 + g4
                    if gg >= 2:
                        pi, pg = divmod(gg - 2, 8)
                        t.wait_ge(Ve, vat(pi, 3 + pg))
                    for j in range(4):
                        dcb = g4 * 4 + j
                        ins = t.transpose(ptr[g4 % 2][:, j * P:(j + 1) * P],
                                          xn_tm[tt % 2][:, dcb * P:(dcb + 1) * P], identB[:, :])
                    ins.then_inc(Pe, 1)

        tot.c["L"] += TT
        tot.c["Ve"] += TT * 10
        tot.c["Ac"] += TT
        tot.c["Pe"] += TT * 8
        _phase_barrier(nc, blk, tot)

    if n_dbg_phases <= 5:
        with nc.Block() as blk:
            @blk.sync
            def _(sy):
                for dc in range(DC):
                    sy.dma_start(xn2_dbg[dc], xn2[:, dc * T:(dc + 1) * T]).then_inc(L, 16)
                sy.wait_ge(L, 16 * (tot.c["L"] + DC))
            tot.c["L"] += DC
        return nc

    B0 = dict(tot.c)
    with nc.named_scope("c"), nc.Block() as blk, ExitStack() as pes:
        pk = {m: pes.enter_context(nc.sbuf_tensor(f"pk{m}", [P, 2048], I8))
              for m in (1, 3)}
        sc = {m: [pes.enter_context(nc.sbuf_tensor(f"sc{m}{i}", [P, 64], F32))
                  for i in range(2)] for m in (1, 3)}
        tmpa = pes.enter_context(nc.sbuf_tensor("tmpa", [P, 2048], I8))
        nat = pes.enter_context(nc.sbuf_tensor("natC", [P, D], BF16))
        lhs = {1: [pes.enter_context(nc.sbuf_tensor(f"lhs1{i}", [P, D], BF16)) for i in range(2)],
               3: [pes.enter_context(nc.sbuf_tensor(f"lhs3{i}", [P, D], BF16)) for i in range(2)]}
        asl = pes.enter_context(nc.sbuf_tensor("asl", [P, T], BF16))
        hsb = pes.enter_context(nc.sbuf_tensor("hsb", [P, T], BF16))
        ptr = [pes.enter_context(nc.psum_tensor(f"ptrF{i}", [P, 512], BF16)) for i in range(4)]
        pa = [pes.enter_context(nc.psum_tensor(f"paF{i}", [P, 512], F32)) for i in range(4)]

        ve_idx, pe_idx, ac_idx = {}, {}, {}
        tg_par = {}

        def _sim():
            vc = pc = sc_ = tg = 0

            def vput(k):
                nonlocal vc
                vc += 1
                ve_idx[k] = vc

            def pput(k):
                nonlocal pc, tg
                pc += 1
                pe_idx[k] = pc
                if k[0] == "tr":
                    tg_par[k[1:]] = tg
                    tg += 1

            def aput(k):
                nonlocal sc_
                sc_ += 1
                ac_idx[k] = sc_

            vput(("dq", 1, 0))
            vput(("dq", 3, 0))
            vput(("dq", 1, 1))
            for hc in range(FC):
                if hc + 1 < FC:
                    vput(("dq", 3, hc + 1))
                for tc in range(4):
                    vput(("hm", hc, tc))
                if hc + 2 < FC:
                    vput(("dq", 1, hc + 2))
            for g in range(8):
                aput(("ev", 1, 0, g))
            for g in range(8):
                aput(("ev", 3, 0, g))
            for hc in range(FC):
                for tc in range(4):
                    aput(("silu", hc, tc))
                if hc + 1 < FC:
                    for g in range(8):
                        aput(("ev", 1, hc + 1, g))
                    for g in range(8):
                        aput(("ev", 3, hc + 1, g))
            for g in range(8):
                pput(("tr", 1, 0, g))
            for g in range(8):
                pput(("tr", 3, 0, g))
            for hc in range(FC):
                for tc in range(4):
                    pput(("mmA", hc, tc))
                if hc + 1 < FC:
                    for g in range(8):
                        pput(("tr", 1, hc + 1, g))
                for tc in range(4):
                    pput(("mmB", hc, tc))
                if hc + 1 < FC:
                    for g in range(8):
                        pput(("tr", 3, hc + 1, g))
            return vc, pc, sc_

        VC_TOT, PC_TOT, AC_TOT = _sim()
        tg_list = sorted(tg_par, key=lambda k: tg_par[k])

        def VE(k):
            return B0["Ve"] + ve_idx[k]

        def PEI(k):
            return B0["Pe"] + pe_idx[k]

        def ACI(k):
            return B0["Ac"] + ac_idx[k]

        def dequant(v, m, hc, dst):
            i = hc % 2
            v.tensor_single_scalar(tmpa[:, :], pk[m][:, :], 15, ALU.bitwise_and)
            v.tensor_tensor(pk[m][:, :], pk[m][:, :], tmpa[:, :], ALU.subtract)
            v.tensor_single_scalar(tmpa[:, :], tmpa[:, :], 8, ALU.bitwise_xor)
            in_m = pk[m].ap().rearrange("p (j c) -> p j c", c=64)
            out_m = bass.AP(dst, 0, [[D, P], [P, 32], [1, 64]])
            s_m = bass.AP(sc[m][i], 0, [[64, P], [2, 32], [0, 64]])
            v.scalar_tensor_tensor(out_m, in_m, 0.0625, s_m, ALU.mult, ALU.mult)
            in_l = tmpa.ap().rearrange("p (j c) -> p j c", c=64)
            out_l = bass.AP(dst, 64, [[D, P], [P, 32], [1, 64]])
            s_l = bass.AP(sc[m][i], 1, [[64, P], [2, 32], [0, 64]])
            return v.scalar_tensor_tensor(out_l, in_l, 8, s_l, ALU.subtract, ALU.mult)

        @blk.sync
        def _(sy):
            for hc in range(FC):
                for m in (1, 3):
                    if hc >= 1:
                        sy.wait_ge(Ve, VE(("dq", m, hc - 1)))
                    sy.dma_start(pk[m][:, :],
                                 bass.AP(wf_in[m].tensor, hc * 262144,
                                         [[2048, P], [1, 2048]])).then_inc(L, 16)
                    if hc >= 2:
                        sy.wait_ge(Ve, VE(("dq", m, hc - 2)))
                    sy.dma_start(sc[m][hc % 2][:, :],
                                 bass.AP(sf_in[m].tensor, hc * 8192, [[64, P], [1, 64]])
                                 ).then_inc(L, 16)

        @blk.vector
        def _(v):
            def dq(m, hc):
                v.wait_ge(L, 16 * (B0["L"] + 4 * hc + (2 if m == 1 else 4)))
                if not (m == 1 and hc == 0):
                    prev = ("tr", 3, hc - 1, 7) if m == 1 else ("tr", 1, hc, 7)
                    v.wait_ge(Pe, PEI(prev))
                dequant(v, m, hc, nat).then_inc(Ve, 1)

            dq(1, 0)
            dq(3, 0)
            dq(1, 1)
            for hc in range(FC):
                if hc + 1 < FC:
                    dq(3, hc + 1)
                for tc in range(4):
                    v.wait_ge(Pe, PEI(("mmB", hc, tc)))
                    if hc >= 1:
                        v.wait_ge(G, 16 * (B0["G"] + hc))
                    v.tensor_mul(hsb[:, tc * 512:(tc + 1) * 512],
                                 asl[:, tc * 512:(tc + 1) * 512], pa[tc][:, :]).then_inc(Ve, 1)
                if hc + 2 < FC:
                    dq(1, hc + 2)

        @blk.tensor
        def _(t):
            def trgrp(m, hc):
                for g in range(8):
                    par = tg_par[(m, hc, g)]
                    if par >= 4:
                        pm, phc, pg = tg_list[par - 4]
                        t.wait_ge(Ac, ACI(("ev", pm, phc, pg)))
                    if g == 0:
                        t.wait_ge(Ve, VE(("dq", m, hc)))
                    for j in range(4):
                        dcb = g * 4 + j
                        ins = t.transpose(ptr[par % 4][:, j * P:(j + 1) * P],
                                          nat[:, dcb * P:(dcb + 1) * P], identB[:, :])
                    ins.then_inc(Pe, 1)

            trgrp(1, 0)
            trgrp(3, 0)
            for hc in range(FC):
                t.wait_ge(Ac, ACI(("ev", 1, hc, 7)))
                if hc >= 1:
                    t.wait_ge(Ve, VE(("hm", hc - 1, 3)))
                for dc in range(DC):
                    for tc in range(4):
                        ins = t.matmul(pa[tc][:, :], lhs[1][hc % 2][:, dc * P:(dc + 1) * P],
                                       xn2[:, dc * T + tc * 512:dc * T + (tc + 1) * 512],
                                       start=(dc == 0), stop=(dc == DC - 1))
                        if dc == DC - 1:
                            ins.then_inc(Pe, 1)
                if hc + 1 < FC:
                    trgrp(1, hc + 1)
                t.wait_ge(Ac, ACI(("ev", 3, hc, 7)))
                t.wait_ge(Ac, ACI(("silu", hc, 3)))
                for dc in range(DC):
                    for tc in range(4):
                        ins = t.matmul(pa[tc][:, :], lhs[3][hc % 2][:, dc * P:(dc + 1) * P],
                                       xn2[:, dc * T + tc * 512:dc * T + (tc + 1) * 512],
                                       start=(dc == 0), stop=(dc == DC - 1))
                        if dc == DC - 1:
                            ins.then_inc(Pe, 1)
                if hc + 1 < FC:
                    trgrp(3, hc + 1)

        @blk.scalar
        def _(s):
            def evs(m, hc):
                for g in range(8):
                    s.wait_ge(Pe, PEI(("tr", m, hc, g)))
                    s.activation(lhs[m][hc % 2][:, g * 512:(g + 1) * 512],
                                 ptr[tg_par[(m, hc, g)] % 4][:, :],
                                 AF.Copy).then_inc(Ac, 1)

            evs(1, 0)
            evs(3, 0)
            for hc in range(FC):
                for tc in range(4):
                    s.wait_ge(Pe, PEI(("mmA", hc, tc)))
                    if hc >= 1:
                        s.wait_ge(Ve, VE(("hm", hc - 1, tc)))
                    s.activation(asl[:, tc * 512:(tc + 1) * 512], pa[tc][:, :],
                                 AF.Silu).then_inc(Ac, 1)
                if hc + 1 < FC:
                    evs(1, hc + 1)
                    evs(3, hc + 1)

        @blk.gpsimd
        def _(g):
            for hc in range(FC):
                g.wait_ge(Ve, VE(("hm", hc, 3)))
                g.dma_start(h_s[hc], hsb[:, :]).then_inc(G, 16)

        tot.c["L"] += 4 * FC
        tot.c["G"] += FC
        tot.c["Pe"] += PC_TOT
        tot.c["Ve"] += VC_TOT
        tot.c["Ac"] += AC_TOT
        _phase_barrier(nc, blk, tot)

    es_xn2.close()
    if n_dbg_phases <= 6:
        return nc

    B0 = dict(tot.c)
    NTB = 4
    NG = 22
    NHB = 8
    NI_D = NTB * DC
    with nc.named_scope("d"), nc.Block() as blk, ExitStack() as pes:
        hblk = pes.enter_context(nc.sbuf_tensor("hblk", [P, FC * 512], BF16))
        w2tb = [pes.enter_context(nc.sbuf_tensor(f"w2tb{i}", [P, FH], BF16)) for i in range(2)]
        pk2 = [pes.enter_context(nc.sbuf_tensor(f"pk2{i}", [P, 5504], I8)) for i in range(2)]
        sc2 = [pes.enter_context(nc.sbuf_tensor(f"sc2{i}", [P, 172], F32)) for i in range(2)]
        tmp8 = pes.enter_context(nc.sbuf_tensor("tmp8", [P, 5504], I8))
        nat2 = pes.enter_context(nc.sbuf_tensor("nat2", [P, FH], BF16))
        ysb = [pes.enter_context(nc.sbuf_tensor(f"ysb{i}", [P, 512], F32)) for i in range(2)]
        x2c = [pes.enter_context(nc.sbuf_tensor(f"x2c{i}", [P, 512], F32)) for i in range(2)]
        osb = [pes.enter_context(nc.sbuf_tensor(f"osb{i}", [P, 512], F32)) for i in range(2)]
        pa = [pes.enter_context(nc.psum_tensor(f"paH{i}", [P, 512], F32)) for i in range(2)]
        ptf = [pes.enter_context(nc.psum_tensor(f"ptfH{i}", [P, 512], F32)) for i in range(2)]
        ptr2 = [pes.enter_context(nc.psum_tensor(f"ptr2{i}", [P, 512], BF16)) for i in range(4)]

        l_i, pe_i, ve_i, g_i, ac_i = {}, {}, {}, {}, {}

        def _mk(d):
            c = [0]

            def put(k):
                c[0] += 1
                d[k] = c[0]
            return put, c
        lput, lc = _mk(l_i)
        pput, pc = _mk(pe_i)
        vput, vc = _mk(ve_i)
        gput, gc = _mk(g_i)
        aput, ac = _mk(ac_i)
        for tb in range(NTB):
            for q in range(NHB):
                lput(("hblk", tb, q))
            for dc in range(DC):
                idx = tb * DC + dc
                if tb == 0:
                    lput(("pk", dc))
                    lput(("sc", dc))
                else:
                    lput(("w2", idx))
                lput(("x2", idx))
        for tb in range(NTB):
            for dc in range(DC):
                idx = tb * DC + dc
                if tb == 0:
                    for g in range(NG):
                        pput(("w2tr", dc, g))
                pput(("mm", idx))
                if idx >= 1:
                    pput(("ytr", idx - 1))
        pput(("ytr", NI_D - 1))
        vput(("dq", 0))
        for dc in range(DC):
            if dc + 1 < DC:
                vput(("dq", dc + 1))
            if dc >= 1:
                vput(("add", dc - 1))
        for tb in range(1, NTB):
            for dc in range(DC):
                vput(("add", tb * DC + dc - 1))
        vput(("add", NI_D - 1))
        for dc in range(DC):
            gput(("strip", dc))
            if dc >= 1:
                gput(("osb", dc - 1))
        for tb in range(1, NTB):
            for dc in range(DC):
                gput(("osb", tb * DC + dc - 1))
        gput(("osb", NI_D - 1))
        for dc in range(DC):
            for g in range(NG):
                aput(("evac", dc, g))
            aput(("yev", dc))
        for idx in range(DC, NI_D):
            aput(("yev", idx))

        def LI(k):
            return 16 * (B0["L"] + l_i[k])

        def PEI(k):
            return B0["Pe"] + pe_i[k]

        def VEI(k):
            return B0["Ve"] + ve_i[k]

        def GI(k):
            return 16 * (B0["G"] + g_i[k])

        def ACI(k):
            return B0["Ac"] + ac_i[k]

        @blk.sync
        def _(sy):
            for tb in range(NTB):
                for q in range(NHB):
                    n = FC // NHB + (1 if q < FC % NHB else 0)
                    h0 = sum(FC // NHB + (1 if qq < FC % NHB else 0) for qq in range(q))
                    if tb >= 1:
                        sy.wait_ge(Pe, PEI(("mm", tb * DC - 1)))
                    src = bass.AP(h_s.tensor, h0 * P * T + tb * 512,
                                  [[T, P], [P * T, n], [1, 512]])
                    dst = bass.AP(hblk, h0 * 512, [[FC * 512, P], [512, n], [1, 512]])
                    sy.dma_start(dst, src).then_inc(L, 16)
                for dc in range(DC):
                    idx = tb * DC + dc
                    if tb == 0:
                        if dc >= 2:
                            sy.wait_ge(Ve, VEI(("dq", dc - 2)))
                        sy.dma_start(pk2[dc % 2][:, :],
                                     bass.AP(wf_in[2].tensor, dc * 704512,
                                             [[5504, P], [1, 5504]])).then_inc(L, 16)
                        sy.dma_start(sc2[dc % 2][:, :],
                                     bass.AP(sf_in[2].tensor, dc * 22016,
                                             [[172, P], [1, 172]])).then_inc(L, 16)
                    else:
                        sy.wait_ge(Pe, PEI(("mm", idx - 2)))
                        sy.wait_ge(G, GI(("strip", dc)))
                        sy.dma_start(w2tb[idx % 2][:, :], w2t_s[dc]).then_inc(L, 16)
                    if idx >= 2:
                        sy.wait_ge(Ve, VEI(("add", idx - 2)))
                    src = bass.AP(x2_s.tensor, (tb * 512) * D + dc * P,
                                  [[D, P], [P * D, 4], [1, P]])
                    sy.dma_start(x2c[idx % 2].ap().rearrange("p (j c) -> p j c", c=P),
                                 src).then_inc(L, 16)

        @blk.vector
        def _(v):
            def dq2(dc):
                v.wait_ge(L, LI(("sc", dc)))
                if dc >= 1:
                    v.wait_ge(Pe, PEI(("w2tr", dc - 1, NG - 1)))
                i = dc % 2
                v.tensor_single_scalar(tmp8[:, :], pk2[i][:, :], 15, ALU.bitwise_and)
                v.tensor_tensor(pk2[i][:, :], pk2[i][:, :], tmp8[:, :], ALU.subtract)
                v.tensor_single_scalar(tmp8[:, :], tmp8[:, :], 8, ALU.bitwise_xor)
                in_m = pk2[i].ap().rearrange("p (j c) -> p j c", c=64)
                out_m = bass.AP(nat2, 0, [[FH, P], [P, 86], [1, 64]])
                s_m = bass.AP(sc2[i], 0, [[172, P], [2, 86], [0, 64]])
                v.scalar_tensor_tensor(out_m, in_m, 0.0625, s_m, ALU.mult, ALU.mult)
                in_l = tmp8.ap().rearrange("p (j c) -> p j c", c=64)
                out_l = bass.AP(nat2, 64, [[FH, P], [P, 86], [1, 64]])
                s_l = bass.AP(sc2[i], 1, [[172, P], [2, 86], [0, 64]])
                v.scalar_tensor_tensor(out_l, in_l, 8, s_l, ALU.subtract, ALU.mult
                                       ).then_inc(Ve, 1)

            def vadd(idx):
                v.wait_ge(Pe, PEI(("ytr", idx)))
                v.wait_ge(L, LI(("x2", idx)))
                if idx >= 2:
                    v.wait_ge(G, GI(("osb", idx - 2)))
                v.tensor_add(osb[idx % 2][:, :], ptf[idx % 2][:, :], x2c[idx % 2][:, :]
                             ).then_inc(Ve, 1)

            dq2(0)
            for dc in range(DC):
                if dc + 1 < DC:
                    dq2(dc + 1)
                if dc >= 1:
                    vadd(dc - 1)
            for tb in range(1, NTB):
                for dc in range(DC):
                    vadd(tb * DC + dc - 1)
            vadd(NI_D - 1)

        @blk.tensor
        def _(t):
            for tb in range(NTB):
                t.wait_ge(L, LI(("hblk", tb, NHB - 1)))
                for dc in range(DC):
                    idx = tb * DC + dc
                    if tb == 0:
                        t.wait_ge(Ve, VEI(("dq", dc)))
                        for g in range(NG):
                            tg = dc * NG + g
                            if tg >= 4:
                                pdc, pg = divmod(tg - 4, NG)
                                t.wait_ge(Ac, ACI(("evac", pdc, pg)))
                            nb = min(4, FC - g * 4)
                            for j in range(nb):
                                b = g * 4 + j
                                ins = t.transpose(ptr2[tg % 4][:, j * P:(j + 1) * P],
                                                  nat2[:, b * P:(b + 1) * P], identB[:, :])
                            ins.then_inc(Pe, 1)
                        t.wait_ge(Ac, ACI(("evac", dc, NG - 1)))
                    else:
                        t.wait_ge(L, LI(("w2", idx)))
                    if idx >= 2:
                        t.wait_ge(Ac, ACI(("yev", idx - 2)))
                    for hc in range(FC):
                        ins = t.matmul(pa[idx % 2][:, :], w2tb[idx % 2][:, hc * P:(hc + 1) * P],
                                       hblk[:, hc * 512:(hc + 1) * 512],
                                       start=(hc == 0), stop=(hc == FC - 1))
                    ins.then_inc(Pe, 1)
                    if idx >= 1:
                        pidx = idx - 1
                        t.wait_ge(Ac, ACI(("yev", pidx)))
                        if pidx >= 2:
                            t.wait_ge(Ve, VEI(("add", pidx - 2)))
                        for j in range(4):
                            ins = t.transpose(ptf[pidx % 2][:, j * P:(j + 1) * P],
                                              ysb[pidx % 2][:, j * P:(j + 1) * P], identF[:, :])
                        ins.then_inc(Pe, 1)
            pidx = NI_D - 1
            t.wait_ge(Ac, ACI(("yev", pidx)))
            t.wait_ge(Ve, VEI(("add", pidx - 2)))
            for j in range(4):
                ins = t.transpose(ptf[pidx % 2][:, j * P:(j + 1) * P],
                                  ysb[pidx % 2][:, j * P:(j + 1) * P], identF[:, :])
            ins.then_inc(Pe, 1)

        @blk.scalar
        def _(s):
            def yev(idx):
                s.wait_ge(Pe, PEI(("mm", idx)))
                if idx >= 2:
                    s.wait_ge(Pe, PEI(("ytr", idx - 2)))
                s.activation(ysb[idx % 2][:, :], pa[idx % 2][:, :], AF.Copy).then_inc(Ac, 1)

            for dc in range(DC):
                for g in range(NG):
                    s.wait_ge(Pe, PEI(("w2tr", dc, g)))
                    if g == 0 and dc >= 2:
                        s.wait_ge(G, GI(("strip", dc - 2)))
                        s.wait_ge(Pe, PEI(("mm", dc - 2)))
                    nb = min(4, FC - g * 4)
                    src = bass.AP(ptr2[(dc * NG + g) % 4], 0, [[512, P], [P, nb], [1, P]])
                    dst = bass.AP(w2tb[dc % 2], g * 4 * P, [[FH, P], [P, nb], [1, P]])
                    s.activation(dst, src, AF.Copy).then_inc(Ac, 1)
                yev(dc)
            for idx in range(DC, NI_D):
                yev(idx)

        @blk.gpsimd
        def _(g):
            def osb_store(idx):
                tb, dc = divmod(idx, DC)
                g.wait_ge(Ve, VEI(("add", idx)))
                dst = bass.AP(out_ap.tensor, (tb * 512) * D + dc * P,
                              [[D, P], [P * D, 4], [1, P]])
                g.dma_start(dst, osb[idx % 2].ap().rearrange("p (j c) -> p j c", c=P)
                            ).then_inc(G, 16)

            for dc in range(DC):
                g.wait_ge(Ac, ACI(("evac", dc, NG - 1)))
                g.dma_start(w2t_s[dc], w2tb[dc % 2][:, :]).then_inc(G, 16)
                if dc >= 1:
                    osb_store(dc - 1)
            for tb in range(1, NTB):
                for dc in range(DC):
                    osb_store(tb * DC + dc - 1)
            osb_store(NI_D - 1)

        tot.c["L"] += lc[0]
        tot.c["G"] += gc[0]
        tot.c["Pe"] += pc[0]
        tot.c["Ve"] += vc[0]
        tot.c["Ac"] += ac[0]
        _phase_barrier(nc, blk, tot)

    return nc


_PER_CORE_KEYS = None


def _tile_wT(w):
    h = np.asarray(w, np.float32).reshape(32, 128, 32, 128)
    return np.ascontiguousarray(h.transpose(0, 3, 2, 1)).reshape(4096, 4096)


def _pack_i8(w):
    return np.ascontiguousarray(np.asarray(w, np.int32).view(np.int8)[:, 0::4])


def kernel(L0_attn_x, L0_attn_Wq, L0_attn_bq, L0_attn_Wk, L0_attn_bk,
           L0_attn_Wv, L0_attn_bv, L0_attn_Wo, L0_attn_bo,
           L0_ffn_w1, L0_ffn_s1, L0_ffn_w2, L0_ffn_s2, L0_ffn_w3, L0_ffn_s3):
    x = np.ascontiguousarray(np.asarray(L0_attn_x, dtype=np.float32))
    shared = {
        "wq": _tile_wT(L0_attn_Wq),
        "wk": _tile_wT(L0_attn_Wk),
        "wv": _tile_wT(L0_attn_Wv),
        "wo": _tile_wT(L0_attn_Wo),
        "bq": np.ascontiguousarray(np.asarray(L0_attn_bq, np.float32)),
        "bk": np.ascontiguousarray(np.asarray(L0_attn_bk, np.float32)),
        "bv": np.ascontiguousarray(np.asarray(L0_attn_bv, np.float32)),
        "bo": np.ascontiguousarray(np.asarray(L0_attn_bo, np.float32)),
        "wf1": _pack_i8(L0_ffn_w1),
        "wf2": _pack_i8(L0_ffn_w2),
        "wf3": _pack_i8(L0_ffn_w3),
        "sf1": np.ascontiguousarray(np.asarray(L0_ffn_s1, np.float32)),
        "sf2": np.ascontiguousarray(np.asarray(L0_ffn_s2, np.float32)),
        "sf3": np.ascontiguousarray(np.asarray(L0_ffn_s3, np.float32)),
    }
    nc = build()
    in_maps = [dict(shared, x=np.ascontiguousarray(x[c])) for c in range(8)]
    res = run_bass_kernel_spmd(nc, in_maps, core_ids=list(range(8)))
    return np.stack([r["out"] for r in res.results], axis=0)
